# version 03 (final)
# baseline (speedup 1.0000x reference)
import math
import numpy as np
import ml_dtypes
import concourse.bass as bass
import concourse.mybir as mybir
from concourse.bass_utils import run_bass_kernel_spmd

F32 = mybir.dt.float32
BF16 = mybir.dt.bfloat16
U8 = mybir.dt.uint8
AF = mybir.ActivationFunctionType
ALU = mybir.AluOpType
BF = ml_dtypes.bfloat16

B, T, D = 4, 2048, 4096
NH, HD = 32, 128
KC = 32
SCALE = 1.0 / math.sqrt(HD)
SHIFT = -10.0
NEG = -1.0e9
NE = 3277
EPS = 1e-5
NCORES = 8
ENGS = ("sync", "tensor", "vector", "scalar", "gpsimd")
DBG_INTERNALS = ()


class Prog:

    def __init__(self):
        self.seq = {e: [] for e in ENGS}
        self.cnt = {s: 0 for s in ("ld", "sp", "pe", "vec", "sc", "bar")}

    def wait(self, eng, sem, val):
        assert val <= self.cnt[sem], (eng, sem, val, self.cnt[sem])
        if val > 0:
            self.seq[eng].append(("w", sem, val))

    def wt(self, eng, tok):
        if tok is not None:
            sem, val = tok
            if sem in ("ld", "sp"):
                val = self.cnt[sem]
            self.wait(eng, sem, val)

    def fence(self, eng, sem):
        self.wait(eng, sem, self.cnt[sem])

    def op(self, eng, fn, inc=None):
        incs = []
        if inc is not None:
            self.cnt[inc] += 1
            incs.append((inc, 1))
        self.seq[eng].append(("o", fn, incs))
        return (inc, self.cnt[inc]) if inc is not None else None

    def dma(self, eng, sem, out_ap, in_ap):
        self.cnt[sem] += 16
        tok = (sem, self.cnt[sem])
        self.seq[eng].append(
            ("o", lambda e, o=out_ap, i=in_ap: e.dma_start(out=o, in_=i),
             [(sem, 16)]))
        return tok

    def tok(self, sem):
        return (sem, self.cnt[sem])

    def barrier(self):
        self.wait("sync", "ld", self.cnt["ld"])
        self.wait("gpsimd", "sp", self.cnt["sp"])
        for e in ENGS:
            self.cnt["bar"] += 1
            self.seq[e].append(("o", lambda eng: eng.sem_inc_hack("bar"), []))
        tgt = self.cnt["bar"]
        for e in ENGS:
            self.wait(e, "bar", tgt)


def build_program():
    nc = bass.Bass("TRN2", target_bir_lowering=False)
    p = Prog()

    xg = nc.dram_tensor("xg", [16, 512, D], F32, kind="ExternalInput").ap()
    wq = nc.dram_tensor("wq", [8, KC, 128, 512], F32, kind="ExternalInput").ap()
    wkv = nc.dram_tensor("wkv", [KC, 128, 512], F32, kind="ExternalInput").ap()
    wo = nc.dram_tensor("wo", [8, KC, 128, 512], F32, kind="ExternalInput").ap()
    wffn = [nc.dram_tensor(f"w{l}", [8, KC, 128, 512], U8,
                           kind="ExternalInput").ap() for l in range(3)]
    bo_t = nc.dram_tensor("bo_t", [128, KC], F32, kind="ExternalInput").ap()
    mvec_t = nc.dram_tensor("mvec_t", [128, KC], F32, kind="ExternalInput").ap()
    svec_in = nc.dram_tensor("svec", [1, 3], F32, kind="ExternalInput").ap()
    ident_bf = nc.dram_tensor("ident_bf", [128, 128], BF16,
                              kind="ExternalInput").ap()
    ident_f32 = nc.dram_tensor("ident_f32", [128, 128], F32,
                               kind="ExternalInput").ap()
    maskneg = nc.dram_tensor("maskneg", [2, 16, 128, 512], F32,
                             kind="ExternalInput").ap()
    out_f = nc.dram_tensor("out_f", [D, 1024], F32, kind="ExternalOutput").ap()

    def KI(nm):
        return "ExternalOutput" if nm in DBG_INTERNALS else "Internal"

    d_qhi = nc.dram_tensor("d_qhi", [D, 1024], BF16, kind=KI("d_qhi")).ap()
    d_qlo = nc.dram_tensor("d_qlo", [D, 1024], BF16, kind=KI("d_qlo")).ap()
    d_khi = nc.dram_tensor("d_khi", [2, 4, 128, T], BF16,
                           kind=KI("d_khi")).ap()
    d_klo = nc.dram_tensor("d_klo", [2, 4, 128, T], BF16,
                           kind=KI("d_klo")).ap()
    d_vhi = nc.dram_tensor("d_vhi", [2, 4, 16, 128, 128], BF16,
                           kind=KI("d_vhi")).ap()
    d_vlo = nc.dram_tensor("d_vlo", [2, 4, 16, 128, 128], BF16,
                           kind=KI("d_vlo")).ap()
    d_chi = nc.dram_tensor("d_chi", [D, 1024], BF16, kind=KI("d_chi")).ap()
    d_clo = nc.dram_tensor("d_clo", [D, 1024], BF16, kind=KI("d_clo")).ap()
    d_xt = nc.dram_tensor("d_xt", [D, 1024], F32, kind=KI("d_xt")).ap()
    d_xn1 = nc.dram_tensor("d_xn1", [D, 1024], F32, kind=KI("d_xn1")).ap()
    d_mark = nc.dram_tensor("d_mark", [128, 1], F32, kind="Internal").ap()

    sb_ident = nc.alloc_sbuf_tensor("sb_ident", [128, 128], BF16).ap()
    sb_identf = nc.alloc_sbuf_tensor("sb_identf", [128, 128], F32).ap()
    sb_bo = nc.alloc_sbuf_tensor("sb_bo", [128, KC], F32).ap()
    sb_mv = nc.alloc_sbuf_tensor("sb_mv", [128, KC], F32).ap()
    sb_sv = nc.alloc_sbuf_tensor("sb_sv", [1, 3], F32).ap()
    sb_onc = nc.alloc_sbuf_tensor("sb_onc", [128, 1], F32).ap()
    sb_oncb = nc.alloc_sbuf_tensor("sb_oncb", [128, 1], BF16).ap()
    sb_onr = nc.alloc_sbuf_tensor("sb_onr", [1, 128], F32).ap()
    sb_s1x = nc.alloc_sbuf_tensor("sb_s1x", [128, 3], F32).ap()
    sb_s2x = nc.alloc_sbuf_tensor("sb_s2x", [128, 3], F32).ap()
    sb_s4x = nc.alloc_sbuf_tensor("sb_s4x", [128, 3], F32).ap()
    sb_s2r = nc.alloc_sbuf_tensor("sb_s2r", [1, 3], F32).ap()
    sb_stv = nc.alloc_sbuf_tensor("sb_stv", [1, 6 * 512], F32).ap()
    sb_cp = nc.alloc_sbuf_tensor("sb_cp", [128, 4], F32).ap()
    sb_mark = nc.alloc_sbuf_tensor("sb_mark", [128, 1], F32).ap()
    nc.const_aps.register(F32, 0.0, sb_cp[:, 0:1])
    nc.const_aps.register(F32, 1.0, sb_cp[:, 1:2])
    nc.const_aps.register(F32, SHIFT, sb_cp[:, 2:3])
    nc.const_aps.register(F32, EPS, sb_cp[:, 3:4])

    PS_h = nc.alloc_psum_tensor("PS", [128, 8 * 512], F32)
    PS = PS_h.ap()

    def bank(i, rows=128):
        return PS[0:rows, i * 512:(i + 1) * 512]

    def bslice(i, c0, c1, rows=128):
        return PS[0:rows, i * 512 + c0: i * 512 + c1]

    semh = {}

    def mark(eng):
        return p.dma(eng, "ld", sb_mark[:], d_mark[:])

    class EngWrap:
        def __init__(self, eng):
            self._e = eng

        def __getattr__(self, n):
            return getattr(self._e, n)

        def sem_inc_hack(self, name):
            return self._e.sem_inc(semh[name], 1)

    p.dma("sync", "ld", sb_ident[:], ident_bf[:])
    p.dma("sync", "ld", sb_identf[:], ident_f32[:])
    p.dma("sync", "ld", sb_bo[:], bo_t[:])
    p.dma("sync", "ld", sb_mv[:], mvec_t[:])
    tok_sv = p.dma("sync", "ld", sb_sv[:], svec_in[:])
    p.wt("vector", tok_sv)
    p.op("vector", lambda e: e.memset(sb_cp[:, 0:1], 0.0))
    p.op("vector", lambda e: e.memset(sb_cp[:, 1:2], 1.0))
    p.op("vector", lambda e: e.memset(sb_cp[:, 2:3], SHIFT))
    t_const = p.op("vector", lambda e: e.memset(sb_cp[:, 3:4], EPS),
                   inc="vec")
    p.wt("scalar", t_const)
    p.op("vector", lambda e: e.memset(sb_onc[:], 1.0))
    p.op("vector", lambda e: e.memset(sb_oncb[:], 1.0))
    p.op("vector", lambda e: e.memset(sb_onr[:], 1.0))
    tv = p.op("vector", lambda e: e.tensor_scalar(
        out=sb_s2r[:], in0=sb_sv[:], scalar1=2.0, scalar2=None, op0=ALU.mult),
        inc="vec")
    p.wt("tensor", tv)
    tpe = p.op("tensor", lambda e: e.matmul(
        bslice(0, 0, 3), sb_onr[:], sb_sv[:], start=True, stop=True),
        inc="pe")
    p.wt("vector", tpe)
    p.op("vector", lambda e: e.tensor_copy(out=sb_s1x[:], in_=bslice(0, 0, 3)))
    p.op("vector", lambda e: e.tensor_scalar(
        out=sb_s2x[:], in0=bslice(0, 0, 3), scalar1=2.0, scalar2=None,
        op0=ALU.mult))
    tv2 = p.op("vector", lambda e: e.tensor_scalar(
        out=sb_s4x[:], in0=bslice(0, 0, 3), scalar1=4.0, scalar2=None,
        op0=ALU.mult), inc="vec")
    bank0_free = tv2

    gA = []

    def asb(name, shape, dt):
        g = nc.sbuf_tensor(name, shape, dt)
        gA.append(g)
        return g.__enter__().ap()

    wkv_raw = [asb(f"A_wkvraw{i}", [128, 512], F32) for i in range(2)]
    wkv_hi = asb("A_wkvhi", [128, KC * 512], BF16)
    wkv_lo = asb("A_wkvlo", [128, KC * 512], BF16)
    xraw2 = [asb(f"A_xraw{i}", [128, D], F32) for i in range(2)]
    xh = [asb(f"A_xh{i}", [128, 128], BF16) for i in range(2)]
    xl = [asb(f"A_xl{i}", [128, 128], BF16) for i in range(2)]
    xt_hi = asb("A_xthi", [128, KC * 512], BF16)
    xt_lo = asb("A_xtlo", [128, KC * 512], BF16)
    kvst = [asb(f"A_kvst{i}", [128, 512], F32) for i in range(2)]
    kv_hi = [asb(f"A_kvhi{i}", [128, 512], BF16) for i in range(2)]
    kv_lo = [asb(f"A_kvlo{i}", [128, 512], BF16) for i in range(2)]
    vt_hi = [asb(f"A_vthi{i}", [128, 128], BF16) for i in range(2)]
    vt_lo = [asb(f"A_vtlo{i}", [128, 128], BF16) for i in range(2)]
    wblk = [asb(f"A_wblk{i}", [128, 512], F32) for i in range(3)]
    wb_hi = [asb(f"A_wbhi{i}", [128, 512], BF16) for i in range(3)]
    wb_lo = [asb(f"A_wblo{i}", [128, 512], BF16) for i in range(3)]
    qst = asb("A_qst", [128, 512], F32)
    q_sthi = asb("A_qsthi", [128, 512], BF16)
    q_stlo = asb("A_qstlo", [128, 512], BF16)
    xtf = [asb(f"A_xtf{i}", [128, 128], F32) for i in range(2)]

    last_wkv = None
    free_wkvraw = [None, None]
    for kc in range(KC):
        r = kc % 2
        p.wt("sync", free_wkvraw[r])
        tl = p.dma("sync", "ld", wkv_raw[r][:], wkv[kc])
        tl = mark("sync")
        p.wt("scalar", tl)
        th = p.op("scalar", lambda e, kc=kc, r=r: e.copy(
            out=wkv_hi[:, kc * 512:(kc + 1) * 512],
            in_=wkv_raw[r][:]), inc="sc")
        p.wt("vector", th)
        last_wkv = p.op("vector", lambda e, kc=kc, r=r: e.tensor_tensor(
            out=wkv_lo[:, kc * 512:(kc + 1) * 512],
            in0=wkv_raw[r][:],
            in1=wkv_hi[:, kc * 512:(kc + 1) * 512], op=ALU.subtract),
            inc="vec")
        free_wkvraw[r] = last_wkv

    free_xraw2 = [[bank0_free], []]
    free_xhl = [None, None]
    free_xtbank = [None, None, None, None]
    free_xtbank[0] = bank0_free
    free_f45 = [None, None]
    free_kvst = [None, None]
    free_kvout = [None, None]
    free_vt = [None, None]
    free_wblk = [None, None, None]
    free_wbhl = [None, None, None]
    free_qst = None
    free_qhl = None
    kv_bank_free = [None, None, None, None]
    q_bank_free = [None, None, None, None]
    xt_evac = {}

    for grp in range(16):
        own = grp < 2
        bslot, u = grp // 4, grp % 4
        for tt in range(4):
            xb_i = (grp * 4 + tt) % 2
            xraw = xraw2[xb_i]
            for t in free_xraw2[xb_i]:
                p.wt("sync", t)
            tl = p.dma("sync", "ld", xraw[:],
                       xg[grp, tt * 128:(tt + 1) * 128, :])
            tl = mark("sync")
            free_xraw2[xb_i] = []
            pend_evac = []
            for kc in range(KC):
                r = kc % 2
                bh, bl = r, 2 + r
                sl = xraw[:, kc * 128:(kc + 1) * 128]
                p.wt("scalar", tl)
                p.wt("scalar", free_xhl[r])
                thx = p.op("scalar", lambda e, sl=sl, r=r: e.copy(
                    out=xh[r][:], in_=sl), inc="sc")
                p.wt("vector", tl)
                p.wt("vector", thx)
                tv = p.op("vector", lambda e, sl=sl, r=r: e.tensor_tensor(
                    out=xl[r][:], in0=sl, in1=xh[r][:], op=ALU.subtract),
                    inc="vec")
                if kc == KC - 1:
                    free_xraw2[xb_i].append(tv)
                p.wt("tensor", tv)
                p.wt("tensor", free_xtbank[bh])
                p.wt("tensor", free_xtbank[bl])
                p.op("tensor", lambda e, r=r, bh=bh, tt=tt: e.matmul(
                    bslice(bh, tt * 128, (tt + 1) * 128), xh[r][:],
                    sb_ident[:], start=True, stop=True))
                tpe = p.op("tensor", lambda e, r=r, bl=bl, tt=tt: e.matmul(
                    bslice(bl, tt * 128, (tt + 1) * 128), xl[r][:],
                    sb_ident[:], start=True, stop=True), inc="pe")
                free_xhl[r] = tpe
                if own:
                    fb = 4 + (kc % 2)
                    p.wt("tensor", free_f45[kc % 2])
                    p.wt("tensor", tl)
                    tpf = p.op("tensor",
                               lambda e, fb=fb,
                               xsl=xraw[:, kc * 128:(kc + 1) * 128]:
                               e.matmul(bslice(fb, 0, 128), xsl,
                                        sb_identf[:], start=True, stop=True),
                               inc="pe")
                    if kc >= KC - 2:
                        free_xraw2[xb_i].append(tpf)
                pend_evac.append((kc, tt, bh, bl))
                if len(pend_evac) > 1:
                    ekc, ett, ebh, ebl = pend_evac.pop(0)
                    p.wt("scalar", tpe)
                    p.op("scalar", lambda e, kc=ekc, tt=ett, bh=ebh: e.copy(
                        out=xt_hi[:, kc * 512 + tt * 128:
                                  kc * 512 + (tt + 1) * 128],
                        in_=bslice(bh, tt * 128, (tt + 1) * 128)))
                    tsc = p.op("scalar", lambda e, kc=ekc, tt=ett, bl=ebl:
                               e.copy(
                        out=xt_lo[:, kc * 512 + tt * 128:
                                  kc * 512 + (tt + 1) * 128],
                        in_=bslice(bl, tt * 128, (tt + 1) * 128)), inc="sc")
                    free_xtbank[ebh] = tsc
                    free_xtbank[ebl] = tsc
                    if ett == 3:
                        xt_evac[ekc] = tsc
                if kc == KC - 1:
                    while pend_evac:
                        ekc, ett, ebh, ebl = pend_evac.pop(0)
                        p.wt("scalar", tpe)
                        p.op("scalar", lambda e, kc=ekc, tt=ett, bh=ebh:
                             e.copy(
                            out=xt_hi[:, kc * 512 + tt * 128:
                                      kc * 512 + (tt + 1) * 128],
                            in_=bslice(bh, tt * 128, (tt + 1) * 128)))
                        tsc = p.op("scalar", lambda e, kc=ekc, tt=ett, bl=ebl:
                                   e.copy(
                            out=xt_lo[:, kc * 512 + tt * 128:
                                      kc * 512 + (tt + 1) * 128],
                            in_=bslice(bl, tt * 128, (tt + 1) * 128)),
                            inc="sc")
                        free_xtbank[ebh] = tsc
                        free_xtbank[ebl] = tsc
                        if ett == 3:
                            xt_evac[ekc] = tsc
                if own:
                    fr = kc % 2
                    p.wt("scalar", tpf)
                    p.wt("scalar", free_vt[fr])
                    tsf = p.op("scalar", lambda e, fb=fb, fr=fr: e.copy(
                        out=xtf[fr][:], in_=bslice(fb, 0, 128)), inc="sc")
                    free_f45[kc % 2] = tsf
                    p.wt("gpsimd", tsf)
                    tsp = p.dma(
                        "gpsimd", "sp",
                        d_xt[kc * 128:(kc + 1) * 128,
                             grp * 512 + tt * 128:grp * 512 + (tt + 1) * 128],
                        xtf[fr][:])
                    free_vt[fr] = tsp
        p.wt("tensor", last_wkv)
        p.wt("tensor", xt_evac[KC - 1])
        kv_mm_tok = [None] * 4
        for kc in range(KC):
            p.wt("tensor", xt_evac[kc])
            if kc == 0:
                for j in range(4):
                    p.wt("tensor", kv_bank_free[j])
            xsl_hi = xt_hi[:, kc * 512:(kc + 1) * 512]
            xsl_lo = xt_lo[:, kc * 512:(kc + 1) * 512]
            for j in range(4):
                whi = wkv_hi[:, kc * 512 + j * 128:kc * 512 + (j + 1) * 128]
                wlo = wkv_lo[:, kc * 512 + j * 128:kc * 512 + (j + 1) * 128]
                dst = bank(j)
                st, sp_ = (kc == 0), (kc == KC - 1)
                p.op("tensor", lambda e, w=whi, x=xsl_hi, d=dst, s=st:
                     e.matmul(d, w, x, start=s, stop=False))
                p.op("tensor", lambda e, w=whi, x=xsl_lo, d=dst:
                     e.matmul(d, w, x, start=False, stop=False))
                tk = p.op("tensor", lambda e, w=wlo, x=xsl_hi, d=dst, s2=sp_:
                          e.matmul(d, w, x, start=False, stop=s2),
                          inc="pe" if sp_ else None)
                if sp_:
                    kv_mm_tok[j] = tk
        for j in range(4):
            gsel, isv = j // 2, j % 2
            r = j % 2
            p.wt("vector", kv_mm_tok[j])
            p.wt("vector", free_kvst[r])
            tv0 = p.op("vector", lambda e, j=j, r=r: e.tensor_copy(
                out=kvst[r][:], in_=bank(j)), inc="vec")
            kv_bank_free[j] = tv0
            p.wt("scalar", tv0)
            p.wt("scalar", free_kvout[r])
            th = p.op("scalar", lambda e, r=r: e.copy(
                out=kv_hi[r][:], in_=kvst[r][:]), inc="sc")
            p.wt("vector", th)
            tl2 = p.op("vector", lambda e, r=r: e.tensor_tensor(
                out=kv_lo[r][:], in0=kvst[r][:], in1=kv_hi[r][:],
                op=ALU.subtract), inc="vec")
            free_kvst[r] = tl2
            if not isv:
                p.wt("gpsimd", tl2)
                p.dma("gpsimd", "sp",
                      d_khi[gsel, bslot][:, u * 512:(u + 1) * 512],
                      kv_hi[r][:])
                tok_l = p.dma("gpsimd", "sp",
                              d_klo[gsel, bslot][:, u * 512:(u + 1) * 512],
                              kv_lo[r][:])
                free_kvout[r] = tok_l
            else:
                last_vt = None
                for tt in range(4):
                    fb = 4 + (tt % 2)
                    p.wt("tensor", tl2)
                    p.wt("tensor", free_f45[tt % 2])
                    p.op("tensor", lambda e, r=r, tt=tt, fb=fb: e.matmul(
                        bslice(fb, 0, 128),
                        kv_hi[r][:, tt * 128:(tt + 1) * 128], sb_ident[:],
                        start=True, stop=True))
                    tpv = p.op("tensor", lambda e, r=r, tt=tt, fb=fb: e.matmul(
                        bslice(fb, 128, 256),
                        kv_lo[r][:, tt * 128:(tt + 1) * 128], sb_ident[:],
                        start=True, stop=True), inc="pe")
                    if tt == 3:
                        free_kvout[r] = tpv
                    vr = tt % 2
                    p.wt("scalar", tpv)
                    p.wt("scalar", free_vt[vr])
                    p.op("scalar", lambda e, vr=vr, fb=fb: e.copy(
                        out=vt_hi[vr][:], in_=bslice(fb, 0, 128)))
                    tse = p.op("scalar", lambda e, vr=vr, fb=fb: e.copy(
                        out=vt_lo[vr][:], in_=bslice(fb, 128, 256)), inc="sc")
                    free_f45[tt % 2] = tse
                    ktile = u * 4 + tt
                    p.wt("gpsimd", tse)
                    p.dma("gpsimd", "sp", d_vhi[gsel, bslot, ktile],
                          vt_hi[vr][:])
                    tvl = p.dma("gpsimd", "sp", d_vlo[gsel, bslot, ktile],
                                vt_lo[vr][:])
                    free_vt[vr] = tvl
        if own:
            for mcb in range(8):
                q_mm_tok = [None] * 4
                for kc in range(KC):
                    bi = (mcb * KC + kc) % 3
                    p.wt("sync", free_wblk[bi])
                    tl = p.dma("sync", "ld", wblk[bi][:], wq[mcb, kc])
                    if mcb == 0 and kc < 3:
                        tl = mark("sync")
                    p.wt("scalar", tl)
                    p.wt("scalar", free_wbhl[bi])
                    th = p.op("scalar", lambda e, bi=bi: e.copy(
                        out=wb_hi[bi][:], in_=wblk[bi][:]), inc="sc")
                    p.wt("vector", th)
                    tw = p.op("vector", lambda e, bi=bi: e.tensor_tensor(
                        out=wb_lo[bi][:], in0=wblk[bi][:], in1=wb_hi[bi][:],
                        op=ALU.subtract), inc="vec")
                    free_wblk[bi] = tw
                    p.wt("tensor", tw)
                    if kc == 0:
                        for mc in range(4):
                            p.wt("tensor", q_bank_free[mc])
                            p.wt("tensor", kv_bank_free[mc])
                    xsl_hi = xt_hi[:, kc * 512:(kc + 1) * 512]
                    xsl_lo = xt_lo[:, kc * 512:(kc + 1) * 512]
                    for mc in range(4):
                        dst = bank(mc)
                        w1 = wb_hi[bi][:, mc * 128:(mc + 1) * 128]
                        w2 = wb_lo[bi][:, mc * 128:(mc + 1) * 128]
                        st, sp_ = (kc == 0), (kc == KC - 1)
                        p.op("tensor", lambda e, w=w1, x=xsl_hi, d=dst, s=st:
                             e.matmul(d, w, x, start=s, stop=False))
                        p.op("tensor", lambda e, w=w1, x=xsl_lo, d=dst:
                             e.matmul(d, w, x, start=False, stop=False))
                        tq = p.op("tensor",
                                  lambda e, w=w2, x=xsl_hi, d=dst, s2=sp_:
                                  e.matmul(d, w, x, start=False, stop=s2),
                                  inc="pe" if (sp_ or mc == 3) else None)
                        if sp_:
                            q_mm_tok[mc] = tq
                    free_wbhl[bi] = tq
                for mc in range(4):
                    p.wt("vector", q_mm_tok[mc])
                    p.wt("vector", free_qst)
                    tv0 = p.op("vector", lambda e, mc=mc: e.tensor_copy(
                        out=qst[:], in_=bank(mc)), inc="vec")
                    q_bank_free[mc] = tv0
                    kv_bank_free[mc] = tv0
                    p.wt("scalar", tv0)
                    p.wt("scalar", free_qhl)
                    th = p.op("scalar", lambda e: e.copy(
                        out=q_sthi[:], in_=qst[:]), inc="sc")
                    p.wt("vector", th)
                    tl2 = p.op("vector", lambda e: e.tensor_tensor(
                        out=q_stlo[:], in0=qst[:], in1=q_sthi[:],
                        op=ALU.subtract), inc="vec")
                    free_qst = tl2
                    rows = slice((mcb * 4 + mc) * 128, (mcb * 4 + mc + 1) * 128)
                    cols = slice(grp * 512, (grp + 1) * 512)
                    p.wt("gpsimd", tl2)
                    p.dma("gpsimd", "sp", d_qhi[rows, cols], q_sthi[:])
                    tqs = p.dma("gpsimd", "sp", d_qlo[rows, cols], q_stlo[:])
                    free_qhl = tqs
        for j in range(4):
            free_xtbank[j] = kv_bank_free[j]

    for g in reversed(gA):
        g.__exit__(None, None, None)
    p.barrier()

    gB = []

    def bsb(name, shape, dt):
        g = nc.sbuf_tensor(name, shape, dt)
        gB.append(g)
        return g.__enter__().ap()

    qb_hi = bsb("B_qhi", [128, NH * 512], BF16)
    qb_lo = bsb("B_qlo", [128, NH * 512], BF16)
    mk = bsb("B_mask", [128, 16 * 512], F32)
    kb_hi = [bsb(f"B_khi{i}", [128, T], BF16) for i in range(2)]
    kb_lo = [bsb(f"B_klo{i}", [128, T], BF16) for i in range(2)]
    vb_hi = [bsb(f"B_vhi{i}", [128, 16 * 128], BF16) for i in range(2)]
    vb_lo = [bsb(f"B_vlo{i}", [128, 16 * 128], BF16) for i in range(2)]
    stm = [bsb(f"B_stm{i}", [128, 512], F32) for i in range(2)]
    ptf = [bsb(f"B_ptf{i}", [128, 512], F32) for i in range(3)]
    pt_hi = [bsb(f"B_pthi{i}", [128, 512], BF16) for i in range(3)]
    pt_lo = [bsb(f"B_ptlo{i}", [128, 512], BF16) for i in range(3)]
    recs = bsb("B_recs", [1, 512], F32)
    ctxf = bsb("B_ctxf", [128, 512], F32)
    ctx_hi = [bsb(f"B_ctxhi{i}", [128, 512], BF16) for i in range(2)]
    ctx_lo = [bsb(f"B_ctxlo{i}", [128, 512], BF16) for i in range(2)]

    free_st = [None, None]
    free_cx = [None, None]
    free_pt = [None, None, None]
    free_S = None
    free_bc = None
    free_ctxf = None
    free_ctxhl = [None, None]
    free_kvb = [None, None]

    for qtb in range(2):
        p.wt("sync", p.tok("pe"))
        tok_q = None
        for h in range(NH):
            rows = slice(h * 128, (h + 1) * 128)
            cols = slice(qtb * 512, (qtb + 1) * 512)
            p.dma("sync", "ld", qb_hi[:, h * 512:(h + 1) * 512],
                  d_qhi[rows, cols])
            tok_q = p.dma("sync", "ld", qb_lo[:, h * 512:(h + 1) * 512],
                          d_qlo[rows, cols])
        tok_q = mark("sync")
        tok_mask = None
        for kt in range(16):
            tok_mask = p.dma("sync", "ld", mk[:, kt * 512:(kt + 1) * 512],
                             maskneg[qtb, kt])
        tok_mask = mark("sync")
        for pair in range(8):
            gsel, bslot = pair // 4, pair % 4
            pr = pair % 2
            p.wt("sync", free_kvb[pr])
            p.dma("sync", "ld", kb_hi[pr][:], d_khi[gsel, bslot])
            p.dma("sync", "ld", kb_lo[pr][:], d_klo[gsel, bslot])
            tok_kv = None
            for kt in range(16):
                p.dma("sync", "ld", vb_hi[pr][:, kt * 128:(kt + 1) * 128],
                      d_vhi[gsel, bslot, kt])
                tok_kv = p.dma("sync", "ld",
                               vb_lo[pr][:, kt * 128:(kt + 1) * 128],
                               d_vlo[gsel, bslot, kt])
            tok_kv = mark("sync")
            for hj in range(4):
                h = gsel * 16 + bslot * 4 + hj
                hb = 2 + (h % 2)
                qsl_hi = qb_hi[:, h * 512:(h + 1) * 512]
                qsl_lo = qb_lo[:, h * 512:(h + 1) * 512]
                last_ctx = None
                last_S = None
                for kt in range(16):
                    sb = kt % 2
                    pb = kt % 3
                    dst = bank(sb)
                    khi = kb_hi[pr][:, kt * 128:(kt + 1) * 128]
                    klo = kb_lo[pr][:, kt * 128:(kt + 1) * 128]
                    p.wt("tensor", tok_kv)
                    p.wt("tensor", tok_q)
                    p.wt("tensor", free_st[sb])
                    p.op("tensor", lambda e, k=khi, q=qsl_hi, d=dst:
                         e.matmul(d, k, q, start=True, stop=False))
                    p.op("tensor", lambda e, k=khi, q=qsl_lo, d=dst:
                         e.matmul(d, k, q, start=False, stop=False))
                    tst = p.op("tensor", lambda e, k=klo, q=qsl_hi, d=dst:
                               e.matmul(d, k, q, start=False, stop=True),
                               inc="pe")
                    p.wt("vector", tst)
                    p.wt("vector", tok_mask)
                    tm = p.op("vector", lambda e, sb=sb, kt=kt:
                              e.tensor_tensor(
                                  out=stm[sb][:], in0=bank(sb),
                                  in1=mk[:, kt * 512:(kt + 1) * 512],
                                  op=ALU.add), inc="vec")
                    free_st[sb] = tm
                    p.wt("scalar", tm)
                    p.wt("scalar", free_pt[pb])
                    te = p.op("scalar", lambda e, sb=sb, pb=pb: e.activation(
                        out=ptf[pb][:], in_=stm[sb][:], func=AF.Exp,
                        bias=SHIFT, scale=SCALE), inc="sc")
                    th = p.op("scalar", lambda e, sb=sb, pb=pb: e.activation(
                        out=pt_hi[pb][:], in_=stm[sb][:], func=AF.Exp,
                        bias=SHIFT, scale=SCALE), inc="sc")
                    p.wt("vector", th)
                    tl2 = p.op("vector", lambda e, pb=pb: e.tensor_tensor(
                        out=pt_lo[pb][:], in0=ptf[pb][:], in1=pt_hi[pb][:],
                        op=ALU.subtract), inc="vec")
                    vhi = vb_hi[pr][:, kt * 128:(kt + 1) * 128]
                    vlo = vb_lo[pr][:, kt * 128:(kt + 1) * 128]
                    cdst = bank(hb)
                    st, sp_ = (kt == 0), (kt == 15)
                    p.wt("tensor", tl2)
                    if st:
                        p.wt("tensor", free_cx[hb - 2])
                        p.wt("tensor", free_S)
                    p.op("tensor", lambda e, v=vhi, t=pt_hi[pb], d=cdst, s=st:
                         e.matmul(d, v, t[:], start=s, stop=False))
                    p.op("tensor", lambda e, v=vlo, t=pt_hi[pb], d=cdst:
                         e.matmul(d, v, t[:], start=False, stop=False))
                    tcx = p.op("tensor",
                               lambda e, v=vhi, t=pt_lo[pb], d=cdst, s2=sp_:
                               e.matmul(d, v, t[:], start=False, stop=s2),
                               inc="pe" if sp_ else None)
                    p.op("tensor", lambda e, t=pt_hi[pb], s=st: e.matmul(
                        bank(4, rows=1), sb_oncb[:], t[:], start=s,
                        stop=False))
                    tS = p.op("tensor", lambda e, t=pt_lo[pb], s2=sp_:
                              e.matmul(bank(4, rows=1), sb_oncb[:], t[:],
                                       start=False, stop=s2), inc="pe")
                    free_pt[pb] = tS
                    if sp_:
                        last_ctx, last_S = tcx, tS
                p.wt("vector", last_S)
                trc = p.op("vector", lambda e: e.reciprocal(
                    out=recs[:], in_=bank(4, rows=1)), inc="vec")
                free_S = trc
                p.wt("tensor", trc)
                p.wt("tensor", free_bc)
                tbc = p.op("tensor", lambda e: e.matmul(
                    bank(5), sb_onr[:], recs[:], start=True, stop=True),
                    inc="pe")
                p.wt("scalar", last_ctx)
                p.wt("scalar", free_ctxf)
                tev = p.op("scalar", lambda e, hb=hb: e.copy(
                    out=ctxf[:], in_=bank(hb)), inc="sc")
                free_cx[hb - 2] = tev
                p.wt("vector", tev)
                p.wt("vector", tbc)
                tcn = p.op("vector", lambda e: e.tensor_tensor(
                    out=ctxf[:], in0=ctxf[:], in1=bank(5), op=ALU.mult),
                    inc="vec")
                free_bc = tcn
                cr = h % 2
                p.wt("scalar", tcn)
                p.wt("scalar", free_ctxhl[cr])
                th2 = p.op("scalar", lambda e, cr=cr: e.copy(
                    out=ctx_hi[cr][:], in_=ctxf[:]), inc="sc")
                p.wt("vector", th2)
                tl3 = p.op("vector", lambda e, cr=cr: e.tensor_tensor(
                    out=ctx_lo[cr][:], in0=ctxf[:], in1=ctx_hi[cr][:],
                    op=ALU.subtract), inc="vec")
                free_ctxf = tl3
                rows = slice(h * 128, (h + 1) * 128)
                cols = slice(qtb * 512, (qtb + 1) * 512)
                p.wt("gpsimd", tl3)
                p.dma("gpsimd", "sp", d_chi[rows, cols], ctx_hi[cr][:])
                tcs = p.dma("gpsimd", "sp", d_clo[rows, cols], ctx_lo[cr][:])
                free_ctxhl[cr] = tcs
            free_kvb[pr] = p.tok("pe")

    for g in reversed(gB):
        g.__exit__(None, None, None)
    p.barrier()

    gC = []

    def csb(name, shape, dt):
        g = nc.sbuf_tensor(name, shape, dt)
        gC.append(g)
        return g.__enter__().ap()

    in_hi = csb("C_inhi", [128, KC * 512], BF16)
    in_lo = csb("C_inlo", [128, KC * 512], BF16)
    resid = csb("C_resid", [128, KC * 512], F32)
    _hview = resid.bitcast(BF16)
    h_hi = _hview[:, 0:KC * 512]
    h_lo = _hview[:, KC * 512:2 * KC * 512]
    wblkC = [csb(f"C_wblk{i}", [128, 512], F32) for i in range(4)]
    wbC_hi = [csb(f"C_wbhi{i}", [128, 512], BF16) for i in range(4)]
    wbC_lo = [csb(f"C_wblo{i}", [128, 512], BF16) for i in range(4)]
    wu8 = [csb(f"C_wu8{i}", [128, 512], U8) for i in range(4)]
    wbfC = [csb(f"C_wbf{i}", [128, 512], BF16) for i in range(4)]
    xres = [csb(f"C_xres{i}", [128, 512], F32) for i in range(4)]
    sqb = [csb(f"C_sq{i}", [128, 512], F32) for i in range(2)]
    hfb = [csb(f"C_hf{i}", [128, 512], F32) for i in range(2)]
    hfb2 = [csb(f"C_hf2{i}", [128, 512], F32) for i in range(2)]
    sb_gapv = csb("C_gapv", [128, 512], F32)
    sb_rsb = csb("C_rsb", [128, 512], F32)

    state = {
        "mm": [None, None, None, None],
        "b4": None, "b5": None, "b6": None, "b7": None,
        "xres": [None, None, None, None],
        "sq": [None, None],
        "hf": [None, None],
        "hfsp": [None, None],
        "hf2": [None, None],
        "rsb": None,
        "w": [None] * 4, "whl": [None] * 4,
        "wu8": [None] * 4, "wbf": [None] * 4,
        "inrhs": None,
        "hrhs": None,
    }

    def emit_mm(w_dram, is_f32w, rhs_hi, rhs_lo, evac_fn):
        last_evac = None
        for mcb in range(8):
            mm_tok = [None] * 4
            for kc in range(KC):
                bi = (mcb * KC + kc) % 4
                if is_f32w:
                    p.wt("sync", state["w"][bi])
                    tl = p.dma("sync", "ld", wblkC[bi][:], w_dram[mcb, kc])
                    if mcb == 0 and kc < 4:
                        tl = mark("sync")
                    p.wt("scalar", tl)
                    p.wt("scalar", state["whl"][bi])
                    th = p.op("scalar", lambda e, bi=bi: e.copy(
                        out=wbC_hi[bi][:], in_=wblkC[bi][:]), inc="sc")
                    p.wt("vector", th)
                    tw = p.op("vector", lambda e, bi=bi: e.tensor_tensor(
                        out=wbC_lo[bi][:], in0=wblkC[bi][:],
                        in1=wbC_hi[bi][:], op=ALU.subtract), inc="vec")
                    state["w"][bi] = tw
                    whi, wlo = wbC_hi[bi], wbC_lo[bi]
                else:
                    p.wt("sync", state["wu8"][bi])
                    tl = p.dma("sync", "ld", wu8[bi][:], w_dram[mcb, kc])
                    if mcb == 0 and kc < 4:
                        tl = mark("sync")
                    p.wt("vector", tl)
                    p.wt("vector", state["wbf"][bi])
                    tw = p.op("vector", lambda e, bi=bi: e.tensor_copy(
                        out=wbfC[bi][:], in_=wu8[bi][:]), inc="vec")
                    state["wu8"][bi] = tw
                    whi, wlo = wbfC[bi], None
                p.wt("tensor", tw)
                if kc == 0:
                    for mc in range(4):
                        p.wt("tensor", state["mm"][mc])
                xsl_hi = rhs_hi[:, kc * 512:(kc + 1) * 512]
                xsl_lo = rhs_lo[:, kc * 512:(kc + 1) * 512]
                for mc in range(4):
                    dst = bank(mc)
                    w1 = whi[:, mc * 128:(mc + 1) * 128]
                    st, sp_ = (kc == 0), (kc == KC - 1)
                    if is_f32w:
                        w2 = wlo[:, mc * 128:(mc + 1) * 128]
                        p.op("tensor", lambda e, w=w1, x=xsl_hi, d=dst, s=st:
                             e.matmul(d, w, x, start=s, stop=False))
                        p.op("tensor", lambda e, w=w1, x=xsl_lo, d=dst:
                             e.matmul(d, w, x, start=False, stop=False))
                        tq = p.op("tensor",
                                  lambda e, w=w2, x=xsl_hi, d=dst, s2=sp_:
                                  e.matmul(d, w, x, start=False, stop=s2),
                                  inc="pe" if (sp_ or mc == 3) else None)
                    else:
                        p.op("tensor", lambda e, w=w1, x=xsl_hi, d=dst, s=st:
                             e.matmul(d, w, x, start=s, stop=False))
                        tq = p.op("tensor",
                                  lambda e, w=w1, x=xsl_lo, d=dst, s2=sp_:
                                  e.matmul(d, w, x, start=False, stop=s2),
                                  inc="pe" if (sp_ or mc == 3) else None)
                    if sp_:
                        mm_tok[mc] = tq
                if is_f32w:
                    state["whl"][bi] = tq
                else:
                    state["wbf"][bi] = tq
            for mc in range(4):
                t = evac_fn(mcb, mc, mm_tok[mc])
                state["mm"][mc] = t
                last_evac = t
        return last_evac

    def emit_ln(src_ready, mvec_on, dst_hi, dst_lo, spill_dram, sub,
                out_dram=None):
        p.wt("tensor", src_ready)
        p.wt("tensor", state["b4"])
        p.wt("tensor", state["b5"])
        for kc in range(KC):
            st, sp_ = (kc == 0), (kc == KC - 1)
            p.op("tensor", lambda e, kc=kc, s=st, s2=sp_: e.matmul(
                bank(4, rows=1), sb_onc[:],
                resid[:, kc * 512:(kc + 1) * 512], start=s, stop=s2))
        tpe = None
        for kc in range(KC):
            r = kc % 2
            st, sp_ = (kc == 0), (kc == KC - 1)
            p.wt("scalar", src_ready)
            p.wt("scalar", state["sq"][r])
            ts = p.op("scalar", lambda e, kc=kc, r=r: e.activation(
                out=sqb[r][:], in_=resid[:, kc * 512:(kc + 1) * 512],
                func=AF.Square), inc="sc")
            p.wt("tensor", ts)
            tpe = p.op("tensor", lambda e, r=r, s=st, s2=sp_: e.matmul(
                bank(5, rows=1), sb_onc[:], sqb[r][:], start=s, stop=s2),
                inc="pe")
            state["sq"][r] = tpe
        p.wt("vector", tpe)
        tmean = p.op("vector", lambda e: e.tensor_scalar(
            out=sb_stv[:, 0:512], in0=bank(4, rows=1), scalar1=1.0 / D,
            scalar2=None, op0=ALU.mult), inc="vec")
        tex2 = p.op("vector", lambda e: e.tensor_scalar(
            out=sb_stv[:, 1024:1536], in0=bank(5, rows=1), scalar1=1.0 / D,
            scalar2=None, op0=ALU.mult), inc="vec")
        p.wt("scalar", tmean)
        tsq = p.op("scalar", lambda e: e.activation(
            out=sb_stv[:, 512:1024], in_=sb_stv[:, 0:512],
            func=AF.Square), inc="sc")
        p.wt("vector", tsq)
        tvv = p.op("vector", lambda e: e.tensor_tensor(
            out=sb_stv[:, 512:1024], in0=sb_stv[:, 1024:1536],
            in1=sb_stv[:, 512:1024], op=ALU.subtract), inc="vec")
        p.wt("scalar", tvv)
        trs = p.op("scalar", lambda e: e.activation(
            out=sb_stv[:, 1536:2048], in_=sb_stv[:, 512:1024], func=AF.Sqrt,
            bias=EPS, scale=1.0), inc="sc")
        p.wt("vector", trs)
        trr = p.op("vector", lambda e: e.reciprocal(
            out=sb_stv[:, 2560:3072], in_=sb_stv[:, 1536:2048]), inc="vec")
        p.wt("tensor", trr)
        p.op("tensor", lambda e: e.matmul(
            bank(4), sb_onr[:], sb_stv[:, 0:512], start=True, stop=True))
        tb2 = p.op("tensor", lambda e: e.matmul(
            bank(5), sb_onr[:], sb_stv[:, 2560:3072], start=True, stop=True),
            inc="pe")
        last = None
        for kc in range(0, KC, 2):
            tap = [None, None]
            for r in range(2):
                p.wt("vector", tb2)
                p.wt("vector", state["hf"][r])
                p.wt("vector", state["hfsp"][r])
                p.op("vector", lambda e, kc=kc + r, r=r: e.tensor_tensor(
                    out=hfb[r][:], in0=resid[:, kc * 512:(kc + 1) * 512],
                    in1=bank(4), op=ALU.subtract))
            for r in range(2):
                tap[r] = p.op("vector", lambda e, r=r: e.tensor_tensor(
                    out=hfb[r][:], in0=hfb[r][:], in1=bank(5), op=ALU.mult),
                    inc="vec")
            for r in range(2):
                kk = kc + r
                ta = tap[r]
                if out_dram is not None:
                    p.wt("gpsimd", ta)
                    tsp = p.dma("gpsimd", "sp",
                                out_dram[kk * 128:(kk + 1) * 128,
                                         sub * 512:(sub + 1) * 512],
                                hfb[r][:])
                    state["hf"][r] = tsp
                    last = ta
                    continue
                tspl = None
                if spill_dram is not None:
                    p.wt("gpsimd", ta)
                    tspl = p.dma("gpsimd", "sp",
                                 spill_dram[kk * 128:(kk + 1) * 128,
                                            sub * 512:(sub + 1) * 512],
                                 hfb[r][:])
                p.wt("scalar", ta)
                th = p.op("scalar", lambda e, kc=kk, r=r: e.copy(
                    out=dst_hi[:, kc * 512:(kc + 1) * 512], in_=hfb[r][:]),
                    inc="sc")
                p.wt("vector", th)
                tl2 = p.op("vector", lambda e, kc=kk, r=r: e.tensor_tensor(
                    out=dst_lo[:, kc * 512:(kc + 1) * 512], in0=hfb[r][:],
                    in1=dst_hi[:, kc * 512:(kc + 1) * 512],
                    op=ALU.subtract), inc="vec")
                state["hf"][r] = tl2
                if tspl is not None:
                    state["hfsp"][r] = tspl
                last = tl2
        state["b4"] = last
        state["b5"] = last
        return last

    def emit_rowsum(rhs_hi, rhs_lo, scale_sl, rhs_ready):
        p.wt("tensor", rhs_ready)
        p.wt("tensor", state["b6"])
        trs = None
        for kc in range(KC):
            st = (kc == 0)
            p.op("tensor", lambda e, kc=kc, s=st: e.matmul(
                bank(6, rows=1), sb_oncb[:],
                rhs_hi[:, kc * 512:(kc + 1) * 512], start=s, stop=False))
            sp_ = (kc == KC - 1)
            trs = p.op("tensor", lambda e, kc=kc, s2=sp_: e.matmul(
                bank(6, rows=1), sb_oncb[:],
                rhs_lo[:, kc * 512:(kc + 1) * 512], start=False, stop=s2),
                inc="pe" if sp_ else None)
        p.wt("vector", trs)
        tsc = p.op("vector", lambda e, ssl=scale_sl: e.tensor_scalar(
            out=sb_stv[:, 2048:2560], in0=bank(6, rows=1), scalar1=ssl,
            scalar2=None, op0=ALU.mult), inc="vec")
        state["b6"] = tsc
        p.wt("tensor", tsc)
        p.wt("tensor", state["b7"])
        tb = p.op("tensor", lambda e: e.matmul(
            bank(7), sb_onr[:], sb_stv[:, 2048:2560], start=True, stop=True),
            inc="pe")
        p.wt("scalar", tb)
        p.wt("scalar", state["rsb"])
        tcp = p.op("scalar", lambda e: e.copy(
            out=sb_rsb[:], in_=bank(7)), inc="sc")
        state["b7"] = tcp
        return tcp

    free_inld = None
    for sub in range(2):
        tok_ctx = None
        p.wt("sync", free_inld)
        for kc in range(KC):
            rows = slice(kc * 128, (kc + 1) * 128)
            cols = slice(sub * 512, (sub + 1) * 512)
            p.dma("sync", "ld", in_hi[:, kc * 512:(kc + 1) * 512],
                  d_chi[rows, cols])
            tok_ctx = p.dma("sync", "ld", in_lo[:, kc * 512:(kc + 1) * 512],
                            d_clo[rows, cols])
        tok_ctx = mark("sync")
        if True:
            pass

        def evac_wo(mcb, mc, mm_tok, sub=sub):
            ch = mcb * 4 + mc
            r = mc
            rows = slice(ch * 128, (ch + 1) * 128)
            cols = slice(sub * 512, (sub + 1) * 512)
            p.wt("sync", state["xres"][r])
            tx = p.dma("sync", "ld", xres[r][:], d_xt[rows, cols])
            p.wt("vector", tx)
            p.wt("vector", mm_tok)
            t1 = p.op("vector", lambda e, ch=ch, r=r, mc=mc:
                      e.scalar_tensor_tensor(
                          out=resid[:, ch * 512:(ch + 1) * 512],
                          in0=bank(mc), scalar=sb_bo[:, ch:ch + 1],
                          in1=xres[r][:], op0=ALU.add, op1=ALU.add),
                      inc="vec")
            state["xres"][r] = t1
            return t1

        p.wt("tensor", tok_ctx)
        t_resid = emit_mm(wo, True, in_hi, in_lo, evac_wo)
        t_split = emit_ln(t_resid, False, in_hi, in_lo, d_xn1, sub)

        for layer in range(3):
            rhs_hi = in_hi if layer != 1 else h_hi
            rhs_lo = in_lo if layer != 1 else h_lo
            dst_hi = h_hi if layer != 1 else in_hi
            dst_lo = h_lo if layer != 1 else in_lo
            ssl = sb_s2r[:, layer:layer + 1] if layer < 2 \
                else sb_sv[:, 2:3]
            trsb = emit_rowsum(rhs_hi, rhs_lo, ssl, t_split)

            if layer < 2:
                pend_lo = []

                def emit_lo(pch, pr, dst_hi=dst_hi, dst_lo=dst_lo):
                    t4 = p.op("vector", lambda e, ch=pch, r=pr:
                              e.scalar_tensor_tensor(
                                  out=dst_lo[:, ch * 512:(ch + 1) * 512],
                                  in0=hfb2[r][:], scalar=sb_mv[:, ch:ch + 1],
                                  in1=dst_hi[:, ch * 512:(ch + 1) * 512],
                                  op0=ALU.mult, op1=ALU.subtract), inc="vec")
                    state["hf2"][pr] = t4
                    return t4

                def evac_ffn(mcb, mc, mm_tok, layer=layer, trsb=trsb,
                             dst_hi=dst_hi, dst_lo=dst_lo, pend_lo=pend_lo,
                             emit_lo=emit_lo):
                    ch = mcb * 4 + mc
                    r = mc % 2
                    p.wt("vector", mm_tok)
                    p.wt("vector", trsb)
                    p.wt("vector", state["hf"][r])
                    p.wt("vector", state["hfsp"][r])
                    t1 = p.op("vector", lambda e, r=r, mc=mc, layer=layer:
                              e.scalar_tensor_tensor(
                                  out=hfb[r][:], in0=bank(mc),
                                  scalar=sb_s4x[:, layer:layer + 1],
                                  in1=sb_rsb[:], op0=ALU.mult,
                                  op1=ALU.subtract), inc="vec")
                    p.wt("scalar", t1)
                    p.wt("scalar", state["hf2"][r])
                    tr = p.op("scalar", lambda e, r=r: e.activation(
                        out=hfb2[r][:], in_=hfb[r][:], func=AF.Relu),
                        inc="sc")
                    p.wt("vector", tr)
                    tvh = p.op("vector", lambda e, ch=ch, r=r: e.tensor_scalar(
                        out=dst_hi[:, ch * 512:(ch + 1) * 512],
                        in0=hfb2[r][:], scalar1=sb_mv[:, ch:ch + 1],
                        scalar2=None, op0=ALU.mult), inc="vec")
                    if pend_lo:
                        emit_lo(*pend_lo.pop(0))
                    pend_lo.append((ch, r))
                    state["hf"][r] = tr
                    return t1
            else:
                pend_add = []

                def emit_add(pch, pr, prx, ptx):
                    p.wt("vector", ptx)
                    t2 = p.op("vector", lambda e, ch=pch, r=pr, rx=prx:
                              e.tensor_tensor(
                                  out=resid[:, ch * 512:(ch + 1) * 512],
                                  in0=hfb[r][:],
                                  in1=xres[rx][:], op=ALU.add), inc="vec")
                    state["hf"][pr] = t2
                    state["xres"][prx] = t2
                    return t2

                def evac_ffn(mcb, mc, mm_tok, trsb=trsb, sub=sub,
                             pend_add=pend_add, emit_add=emit_add):
                    ch = mcb * 4 + mc
                    r = mc % 2
                    rx = mc
                    rows = slice(ch * 128, (ch + 1) * 128)
                    cols = slice(sub * 512, (sub + 1) * 512)
                    p.wt("sync", state["xres"][rx])
                    tx = p.dma("sync", "ld", xres[rx][:], d_xn1[rows, cols])
                    p.wt("vector", mm_tok)
                    p.wt("vector", trsb)
                    p.wt("vector", state["hf"][r])
                    p.wt("vector", state["hfsp"][r])
                    t1 = p.op("vector", lambda e, r=r, mc=mc:
                              e.scalar_tensor_tensor(
                                  out=hfb[r][:], in0=bank(mc),
                                  scalar=sb_s2x[:, 2:3], in1=sb_rsb[:],
                                  op0=ALU.mult, op1=ALU.subtract), inc="vec")
                    if pend_add:
                        emit_add(*pend_add.pop(0))
                    pend_add.append((ch, r, rx, tx))
                    return t1

            if layer == 2:
                p.fence("sync", "sp")
            t_last = emit_mm(wffn[layer], False, rhs_hi, rhs_lo, evac_ffn)
            state["rsb"] = t_last
            p.op("vector", lambda e: e.memset(sb_gapv[:], 0.0))
            if layer < 2:
                t_split = emit_lo(*pend_lo.pop(0))
            else:
                t_last = emit_add(*pend_add.pop(0))
        emit_ln(t_last, False, None, None, None, sub, out_dram=out_f)
        free_inld = p.tok("pe")

    for g in reversed(gC):
        g.__exit__(None, None, None)

    p.wait("gpsimd", "sp", p.cnt["sp"])
    p.wait("sync", "ld", p.cnt["ld"])

    with (
        nc.semaphore("s_ld") as s_ld,
        nc.semaphore("s_sp") as s_sp,
        nc.semaphore("s_pe") as s_pe,
        nc.semaphore("s_vec") as s_vec,
        nc.semaphore("s_sc") as s_sc,
        nc.semaphore("s_bar") as s_bar,
        nc.Block() as block,
    ):
        semh.update({"ld": s_ld, "sp": s_sp, "pe": s_pe, "vec": s_vec,
                     "sc": s_sc, "bar": s_bar})

        def replay(eng_handle, name):
            w = EngWrap(eng_handle)
            for item in p.seq[name]:
                if item[0] == "w":
                    eng_handle.wait_ge(semh[item[1]], item[2])
                else:
                    inst = item[1](w)
                    for (s, v) in item[2]:
                        inst.then_inc(semh[s], v)

        @block.sync
        def _(sync):
            replay(sync, "sync")

        @block.tensor
        def _(tensor):
            replay(tensor, "tensor")

        @block.vector
        def _(vector):
            replay(vector, "vector")

        @block.scalar
        def _(scalar):
            replay(scalar, "scalar")

        @block.gpsimd
        def _(gpsimd):
            replay(gpsimd, "gpsimd")

    return nc


_NC_CACHE = None


def kernel(L0_attn_x, L0_attn_Wq, L0_attn_Wkv, L0_attn_Wo, L0_attn_bo,
           L0_ffn_k0, L0_ffn_s0, L0_ffn_k1, L0_ffn_s1, L0_ffn_k2, L0_ffn_s2):
    global _NC_CACHE
    x = np.ascontiguousarray(np.asarray(L0_attn_x, dtype=np.float32))
    Wq = np.ascontiguousarray(np.asarray(L0_attn_Wq, dtype=np.float32))
    Wkv = np.ascontiguousarray(np.asarray(L0_attn_Wkv, dtype=np.float32))
    Wo = np.ascontiguousarray(np.asarray(L0_attn_Wo, dtype=np.float32))
    bo = np.ascontiguousarray(np.asarray(L0_attn_bo, dtype=np.float32))
    ks = [np.ascontiguousarray(np.asarray(k)).astype(np.uint8)
          for k in (L0_ffn_k0, L0_ffn_k1, L0_ffn_k2)]
    svec = np.stack([np.float32(L0_ffn_s0), np.float32(L0_ffn_s1),
                     np.float32(L0_ffn_s2)]).reshape(1, 3).astype(np.float32)

    def blocks(w):
        return np.ascontiguousarray(
            w.reshape(KC, 128, 8, 512).transpose(2, 0, 1, 3))

    wffn_s = [blocks(k) for k in ks]
    bo_t = np.ascontiguousarray(bo.reshape(KC, 128).T)
    mvec = np.where(np.arange(D) < NE, 1.0, -4.0).astype(np.float32)
    mvec_t = np.ascontiguousarray(mvec.reshape(KC, 128).T)
    ident_bf = np.eye(128, dtype=BF)
    ident_f32 = np.eye(128, dtype=np.float32)

    in_maps = []
    for c in range(NCORES):
        bc, hc = c // 2, c % 2
        b_of_slot = [bc] + [bb for bb in range(B) if bb != bc]
        kgperm = [2 * hc, 2 * hc + 1, 2 * (1 - hc), 2 * (1 - hc) + 1]
        xg = np.stack([x[b_of_slot[s], kgperm[u] * 512:(kgperm[u] + 1) * 512, :]
                       for s in range(4) for u in range(4)])
        perm = np.array([gsel * 16 + b_of_slot[bslot] * 4 + hj
                         for gsel in range(2) for bslot in range(4)
                         for hj in range(4)])
        colperm = (perm[:, None] * 128 + np.arange(128)[None, :]).ravel()
        wq_s = blocks(np.ascontiguousarray(Wq[:, colperm]))
        wo_s = blocks(np.ascontiguousarray(Wo[colperm, :]))
        wkv_s = np.ascontiguousarray(
            Wkv[:, bc * 512:(bc + 1) * 512].reshape(KC, 128, 512))
        kt_idx = np.arange(16)
        key_pos = (np.array(kgperm)[kt_idx // 4] * 512
                   + (kt_idx % 4) * 128)[None, :, None, None] \
            + np.arange(128)[None, None, :, None]
        q_pos = ((2 * hc + np.arange(2)) * 512)[:, None, None, None] \
            + np.arange(512)[None, None, None, :]
        mneg = np.where(key_pos <= q_pos, 0.0, NEG).astype(np.float32)
        mneg = np.ascontiguousarray(
            np.broadcast_to(mneg, (2, 16, 128, 512)).astype(np.float32))
        in_maps.append({
            "xg": np.ascontiguousarray(xg),
            "wq": wq_s, "wkv": wkv_s, "wo": wo_s,
            "w0": wffn_s[0], "w1": wffn_s[1], "w2": wffn_s[2],
            "bo_t": bo_t, "mvec_t": mvec_t, "svec": svec,
            "ident_bf": ident_bf, "ident_f32": ident_f32,
            "maskneg": mneg,
        })

    if _NC_CACHE is None:
        _NC_CACHE = build_program()
    nc = _NC_CACHE

    res = run_bass_kernel_spmd(nc, in_maps, core_ids=list(range(NCORES)))
    out = np.empty((B, T, D), dtype=np.float32)
    for c in range(NCORES):
        bc, hc = c // 2, c % 2
        of = np.asarray(res.results[c]["out_f"])
        out[bc, hc * 1024:(hc + 1) * 1024, :] = of.T
    return out
